# version 01
# baseline (speedup 1.0000x reference)
import numpy as np
import ml_dtypes
import concourse.bass as bass
import concourse.mybir as mybir
from concourse.bass_utils import run_bass_kernel_spmd
from contextlib import ExitStack

F32 = mybir.dt.float32
BF16 = mybir.dt.bfloat16
AF = mybir.ActivationFunctionType
ALU = mybir.AluOpType
AX = mybir.AxisListType

D, DC, L, TOK = 1024, 8, 4096, 2048
NT, NTB = 16, 32
H, HP, DH, NS = 16, 8, 64, 16
NE, DFF, FB, NFB, TCH, NTC = 8, 4096, 1024, 4, 512, 4
EPS = 1e-5
NEG = -1.0e10
BF = ml_dtypes.bfloat16


class Sem:
    def __init__(self, es, nc, name):
        self.h = es.enter_context(nc.semaphore(name))
        self.n = 0


class Plan:

    def __init__(self, b):
        self.b = b
        self.ops = {k: [] for k in ("tensor", "vector", "scalar", "gpsimd", "sync")}
        self.order = 0

    def op(self, eng, fn, waits=(), inc=None, inck=1):
        w = [(s, v) for (s, v) in waits if v is not None and v > 0]
        val = None
        if inc is not None:
            inc.n += inck
            val = inc.n
        self.ops[eng].append((fn, w, inc, inck, self.order))
        self.order += 1
        return val

    def emit(self):
        b = self.b
        for eng in ("gpsimd", "sync", "tensor", "scalar", "vector"):
            lst = self.ops[eng]
            if not lst:
                continue

            def run(e, lst=lst):
                for fn, w, inc, inck, _ in lst:
                    for s, v in w:
                        e.wait_ge(s.h, v)
                    ins = fn(e)
                    if inc is not None:
                        if isinstance(ins, (list, tuple)):
                            assert inck == 16 * len(ins)
                            for one in ins:
                                one.then_inc(inc.h, 16)
                        else:
                            if ins is None:
                                ins = e.nop()
                            ins.then_inc(inc.h, inck)
            getattr(b.blk, eng)(run)


class OffV:
    def __init__(self, t, off, width):
        self.t, self.off, self.w = t, off, width

    def __getitem__(self, idx):
        p, c = idx
        a = 0 if c.start is None else c.start
        b = self.w if c.stop is None else c.stop
        return self.t[p, self.off + a:self.off + b]


class B:
    def __init__(self, stage=5, dump=()):
        self.stage = stage
        self.dumpreq = set(dump)
        import os
        self.nc = bass.Bass("TRN2", target_bir_lowering=False, debug=False,
                            dynamic_dma_scratch_size=16384,
                            enable_partition_id=False,
                            detect_race_conditions=not os.environ.get("MK_NO_RACEDET"))
        self.es = ExitStack()
        self.dumps = {}
        self._psV = {}

    def sec(self, eng, fn):
        getattr(self.blk, eng)(fn)

    def dump_ap(self, name, shape, dtype):
        ap = self.nc.dram_tensor("dump_" + name, list(shape), dtype,
                                 kind="ExternalOutput").ap()
        self.dumps[name] = ap
        return ap

    def emit_dump(self, name, shape, dtype, src_ap, sem, val):
        import os as _os5
        use_sync = bool(_os5.environ.get("MK_DUMP_SYNC")) and dtype == F32
        dap = self.dump_ap(name, shape, dtype if use_sync else F32)
        O = self.O

        def dd(g):
            g.wait_ge(sem.h, val)
            g.dma_start(out=dap, in_=src_ap).then_inc(O.h, 16)
        self.sec("sync" if use_sync else "gpsimd", dd)
        O.n += 16

    def build(self):
        nc, es = self.nc, self.es

        def din(name, shape):
            return nc.dram_tensor(name, list(shape), F32, kind="ExternalInput").ap()

        self.xb = din("xb", (L, D))
        self.xo = din("xo", (TOK, D))
        self.qpos_d = din("qpos", (128, NS))
        self.masks_d = din("masks", (128, 256))
        self.inw = din("inw", (3 * D, D))
        self.outw = din("outw", (D, D))
        self.ln1g_d = din("ln1g", (1, D))
        self.ln1b_d = din("ln1b", (1, D))
        self.ln2g_d = din("ln2g", (1, D))
        self.ln2b_d = din("ln2b", (1, D))
        self.slopes_d = din("slopes", (1, H))
        self.smear_d = din("smear", (1, H))
        self.rw_d = din("rw", (D, NE))
        self.rb_d = din("rb", (1, NE))
        self.w1_d = din("w1", (NE, D, DFF))
        self.b1_d = din("b1", (NE, DFF))
        self.w2_d = din("w2", (NE, DFF, D))
        self.b2_d = din("b2", (NE, D))
        self.out_d = nc.dram_tensor("out", [TOK, D], F32, kind="ExternalOutput").ap()

        jpos_np = (np.arange(32)[None, :] * 128 + np.arange(128)[:, None]).astype(np.float32)
        Bpat = np.zeros((H, 128), dtype=np.float32)
        for h in range(H):
            lo = 0 if h % 2 == 0 else 64
            Bpat[h, lo:lo + 64] = 1.0
        Apat = np.zeros((H, HP), dtype=np.float32)
        for h in range(H):
            Apat[h, h // 2] = 1.0
        self.c_identbf = nc.inline_tensor(np.eye(128, dtype=BF), "c_identbf")
        self.c_identf = nc.inline_tensor(np.eye(128, dtype=np.float32), "c_identf")
        self.c_onesbf = nc.inline_tensor(np.ones((1, 128), dtype=BF), "c_onesbf")
        self.c_onesf = nc.inline_tensor(np.ones((1, 128), dtype=np.float32), "c_onesf")
        self.c_jpos = nc.inline_tensor(jpos_np, "c_jpos")
        self.c_Bpat = nc.inline_tensor(Bpat, "c_Bpat")
        self.c_Apat = nc.inline_tensor(Apat, "c_Apat")

        for name in ("LD", "WD", "T", "V", "A", "O"):
            setattr(self, name, Sem(es, nc, name))

        self.blk = es.enter_context(nc.Block())

        self.ps_tr = es.enter_context(nc.psum_tensor("g_ps_tr", [128, 1024], BF16))
        self.ps_tr2 = es.enter_context(nc.psum_tensor("g_ps_tr2", [128, 1024], BF16))
        self.psf = [es.enter_context(nc.psum_tensor(f"g_psf{i}", [128, 512], F32))
                    for i in range(6)]

        sb = lambda name, shape, dt: es.enter_context(nc.sbuf_tensor(name, shape, dt))
        self.t_identbf = sb("identbf", [128, 128], BF16)
        self.t_identf = sb("identf", [128, 128], F32)
        self.t_onesbf = sb("onesbf", [1, 128], BF16)
        self.t_onesf = sb("onesf", [1, 128], F32)
        self.t_jpos = sb("jpos", [128, 32], F32)
        self.t_Bpat = sb("Bpat", [16, 128], F32)
        self.t_Apat = sb("Apat", [16, HP], F32)
        self.t_qpos = sb("qposs", [128, NS], F32)
        self.t_masks = sb("maskss", [128, 256], F32)
        self.t_sm = sb("smrow", [1, 2 * H], F32)
        self.t_smT = sb("smT", [16, 2], F32)
        self.t_svA = sb("svA", [16, HP], F32)
        self.t_sv = sb("sv", [128, 2 * HP], F32)
        self.t_srep = sb("srep", [128, 2 * H], F32)
        self.t_alibi = sb("alibi", [128, 32], F32)
        self.t_reps1 = sb("reps1", [128, 2 * D], F32)
        self.t_G = sb("G", [128, NT * NE], F32)
        self.t_ar1 = sb("arena1", [128, DC * L], BF16)
        self.t_s = nc.virtual_sbuf_tensor("s_attn", [128, DC * L], BF16)
        nc.back_virtual_tensor_with(self.t_s, self.t_ar1)
        self.t_ar2 = sb("arena2", [128, HP * TOK], BF16)
        self.t_Qp = nc.virtual_sbuf_tensor("Qp", [128, HP * TOK], BF16)
        nc.back_virtual_tensor_with(self.t_Qp, self.t_ar2)

        self.phase0()
        self.prep_small()

        self.setupA()
        ot_cm = None
        if self.stage >= 1:
            ot_cm = nc.sbuf_tensor("oT", [128, HP * TOK], BF16)
            self.t_oT = ot_cm.__enter__()
            self.t_hT = nc.virtual_sbuf_tensor("hT", [128, DC * L], BF16)
            nc.back_virtual_tensor_with(self.t_hT, self.t_ar1)
            self.setupB()
            self.pairs()
        if self.stage >= 2:
            nc.all_engine_barrier()
            self.outproj()
        if ot_cm is not None:
            ot_cm.__exit__(None, None, None)
        if self.stage >= 3:
            nc.all_engine_barrier()
            self.moe()
        self.finish()
        es.close()
        return nc

    def phase0(self):
        LD, WD = self.LD, self.WD

        def p0_sync(sy):
            pairs = [
                (self.t_identbf[:, :], self.c_identbf.ap()),
                (self.t_identf[:, :], self.c_identf.ap()),
                (self.t_onesbf[:, :], self.c_onesbf.ap()),
                (self.t_onesf[:, :], self.c_onesf.ap()),
                (self.t_jpos[:, :], self.c_jpos.ap()),
                (self.t_Bpat[:, :], self.c_Bpat.ap()),
                (self.t_Apat[:, :], self.c_Apat.ap()),
                (self.t_qpos[:, :], self.qpos_d),
                (self.t_masks[:, :], self.masks_d),
                (self.t_sm[:, 0:H], self.slopes_d),
                (self.t_sm[:, H:2 * H], self.smear_d),
            ]
            for o, i in pairs:
                sy.dma_start(out=o, in_=i).then_inc(LD.h, 16)
        LD.n += 16 * 11
        self.sec("sync", p0_sync)
        self.ld0 = LD.n

    def prep_small(self):
        nc, T, V, A, LD = self.nc, self.T, self.V, self.A, self.LD
        with nc.sbuf_tensor("lnrow1", [1, 2 * D], F32) as t_lnr:
            ps_a, ps_b = self.psf[0], self.psf[1]
            p = Plan(self)
            ld0 = p.op("sync", lambda sy: [
                sy.dma_start(out=t_lnr[:, 0:D], in_=self.ln1g_d),
                sy.dma_start(out=t_lnr[:, D:2 * D], in_=self.ln1b_d)],
                waits=[], inc=LD, inck=32)
            t1 = p.op("tensor", lambda t: (
                t.transpose(ps_a[0:16, 0:1], self.t_sm[0:1, 0:H], self.t_identf[0:1, 0:1]),
                t.transpose(ps_a[0:16, 1:2], self.t_sm[0:1, H:2 * H], self.t_identf[0:1, 0:1]),
                t.matmul(ps_b[:, 0:H], self.t_onesf[0:1, :], self.t_sm[0:1, 0:H],
                         start=True, stop=True))[-1],
                      waits=[(LD, ld0)], inc=self.T)
            a1 = p.op("scalar", lambda sc: (
                sc.copy(self.t_smT[:, 0:1], ps_a[0:16, 0:1]),
                sc.nop(nofuse=True),
                sc.activation(self.t_smT[:, 1:2], ps_a[0:16, 1:2], AF.Sigmoid))[-1],
                      waits=[(T, t1)], inc=self.A)
            v1 = p.op("vector", lambda v: v.tensor_scalar_mul(
                self.t_svA[:, :], self.t_Apat[:, :], self.t_smT[:, 1:2]),
                      waits=[(A, a1)], inc=self.V)
            t2 = p.op("tensor", lambda t: t.matmul(
                ps_a[:, 0:HP], self.t_Bpat[:, :], self.t_svA[:, :],
                start=True, stop=True), waits=[(V, v1)], inc=self.T)
            v2 = p.op("vector", lambda v: (
                v.tensor_copy(self.t_sv[:, 0:HP], ps_a[:, 0:HP]),
                v.tensor_copy(self.t_srep[:, 0:H], ps_b[:, 0:H]),
                [v.nop(nofuse=True) for _ in range(4)],
                v.tensor_scalar(self.t_sv[:, HP:2 * HP], self.t_sv[:, 0:HP], -1.0, 1.0,
                                ALU.mult, ALU.add),
                v.tensor_scalar_mul(self.t_srep[:, H:2 * H], self.t_srep[:, 0:H], -1.0))[-1],
                      waits=[(T, t2)], inc=self.V)
            prev = {}
            for k in range(4):
                ps = ps_a if k % 2 == 0 else ps_b
                w = [(LD, ld0), (V, v2)]
                if k >= 2:
                    w.append((V, prev[k - 2]))
                tk = p.op("tensor", lambda t, k=k, ps=ps: t.matmul(
                    ps[:, 0:512], self.t_onesf[0:1, :],
                    t_lnr[0:1, k * 512:(k + 1) * 512],
                    start=True, stop=True), waits=w, inc=self.T)
                prev[k] = p.op("vector", lambda v, k=k, ps=ps: v.tensor_copy(
                    self.t_reps1[:, k * 512:(k + 1) * 512], ps[:, 0:512]),
                    waits=[(T, tk)], inc=self.V)
            p.emit()
            self.T_prep = self.T.n

    def ln_chain(self, p, tiles, cols, scr):
        V, A = self.V, self.A

        def col(i, k):
            return cols[:, i * 8 + k:i * 8 + k + 1]

        rd = []
        for td in tiles:
            rd += td["ready"]

        def v1(v):
            last = None
            for i, td in enumerate(tiles):
                last = v.reduce_sum(col(i, 0), td["src"], axis=AX.X)
            for i, td in enumerate(tiles):
                last = v.tensor_scalar_mul(col(i, 1), col(i, 0), -1.0 / D)
            for i, td in enumerate(tiles):
                last = v.memset(col(i, 2), 0.0)
            for i, td in enumerate(tiles):
                last = v.tensor_scalar_mul(col(i, 6), col(i, 0), 1.0 / D)
            return last
        vs = p.op("vector", v1, waits=rd, inc=V)

        def a1(sc):
            last = None
            for i, td in enumerate(tiles):
                last = sc.activation(scr[:, i * D:(i + 1) * D], td["src"], AF.Square,
                                     bias=col(i, 1), accum_out=col(i, 2))
            return last
        asq = p.op("scalar", a1, waits=rd + [(V, vs)], inc=A)

        def v2(v):
            last = None
            for i, td in enumerate(tiles):
                last = v.tensor_scalar(col(i, 3), col(i, 2), 1.0 / D, EPS,
                                       ALU.mult, ALU.add)
            return last
        vv = p.op("vector", v2, waits=[(A, asq)], inc=V)

        def a2(sc):
            for i, td in enumerate(tiles):
                sc.activation(col(i, 4), col(i, 3), AF.Ln)
            sc.nop(nofuse=True)
            last = None
            for i, td in enumerate(tiles):
                last = sc.activation(col(i, 5), col(i, 4), AF.Exp, scale=-0.5)
            return last
        ar = p.op("scalar", a2, waits=[(V, vv)], inc=A)

        def v3(v):
            last = None
            if tiles[0].get("gb") is None:
                for i, td in enumerate(tiles):
                    last = v.tensor_scalar(td["out"], td["src"], col(i, 6), col(i, 5),
                                           ALU.subtract, ALU.mult)
            else:
                for i, td in enumerate(tiles):
                    v.tensor_scalar(td["tmp"], td["src"], col(i, 6), col(i, 5),
                                    ALU.subtract, ALU.mult)
                v.nop(nofuse=True)
                for i, td in enumerate(tiles):
                    v.tensor_mul(td["tmp"], td["tmp"], td["gb"][0])
                v.nop(nofuse=True)
                for i, td in enumerate(tiles):
                    last = v.tensor_add(td["out"], td["tmp"], td["gb"][1])
            return last
        return p.op("vector", v3, waits=[(A, ar)], inc=V)

    def transp(self, p, items, ps_list, extra_waits=(), dtype="bf"):
        T, V = self.T, self.V
        ident = self.t_identbf if dtype == "bf" else self.t_identf
        pern = 8 if dtype == "bf" else 4
        per = pern * len(ps_list)
        groups = [items[i:i + per] for i in range(0, len(items), per)]
        lastv = None
        for gi, grp in enumerate(groups):
            used = set()
            w = list(extra_waits) if gi == 0 else []
            for k in range(len(grp)):
                used.add(id(ps_list[(k // pern) % len(ps_list)]))
            for u in used:
                if u in self._psV:
                    w.append((V, self._psV[u]))
            if gi >= 1:
                w.append((V, lastv))

            def tf(t, grp=grp):
                last = None
                for k, (src, dst) in enumerate(grp):
                    ps = ps_list[(k // pern) % len(ps_list)]
                    last = t.transpose(ps[:, (k % pern) * 128:(k % pern + 1) * 128],
                                       src, ident[:, :])
                return last
            tk = p.op("tensor", tf, waits=w, inc=T)

            def vf(v, grp=grp):
                last = None
                for k, (src, dst) in enumerate(grp):
                    ps = ps_list[(k // pern) % len(ps_list)]
                    last = v.tensor_copy(dst, ps[:, (k % pern) * 128:(k % pern + 1) * 128])
                return last
            lastv = p.op("vector", vf, waits=[(T, tk)], inc=V)
            for u in used:
                self._psV[u] = lastv
        return lastv

    def setupA(self):
        nc, LD, WD, T, V = self.nc, self.LD, self.WD, self.T, self.V
        with nc.sbuf_tensor("hioT", [128, DC * TOK], BF16) as t_hioT, \
             nc.sbuf_tensor("xstgA", [128, 4 * D], F32) as t_xstg, \
             nc.sbuf_tensor("xnstgA", [128, 4 * D], BF16) as t_xn, \
             nc.sbuf_tensor("histgA", [128, 4 * D], BF16) as t_hi, \
             nc.sbuf_tensor("sqscrA", [128, 2 * D], F32) as t_sq, \
             nc.sbuf_tensor("lncolsA", [128, 32], F32) as t_cols:
            ps_trA, ps_trB = self.ps_tr, self.ps_tr2
            p = Plan(self)
            import os as _osA
            nsa = int(_osA.environ.get("MK_NSA", NT // 2))
            vdone = {}
            for gi in range(nsa):
                grp = [2 * gi, 2 * gi + 1]
                par = gi % 2
                stg = lambda i, t: t[:, (par * 2 + i) * D:(par * 2 + i + 1) * D]
                w = [(V, vdone.get(gi - 2))] if gi >= 2 else [(self.T, self.T_prep)]

                def ldx(sy, grp=grp, par=par):
                    return [sy.dma_start(
                        out=t_xstg[:, (par * 2 + i) * D:(par * 2 + i + 1) * D],
                        in_=self.xo[tt * 128:(tt + 1) * 128, :])
                        for i, tt in enumerate(grp)]
                ldh = p.op("sync", ldx, waits=w, inc=LD, inck=32)

                tiles0 = [dict(src=stg(i, t_xstg), ready=[(LD, ldh)],
                               out=stg(i, t_xn), gb=None) for i in range(2)]
                v0 = self.ln_chain(p, tiles0, t_cols[:, 0:16], t_sq)
                tiles1 = [dict(src=stg(i, t_xn), ready=[(V, v0)],
                               out=stg(i, t_hi),
                               gb=(self.t_reps1[:, 0:D], self.t_reps1[:, D:2 * D]),
                               tmp=t_sq[:, i * D:(i + 1) * D]) for i in range(2)]
                v1 = self.ln_chain(p, tiles1, t_cols[:, 16:32], t_sq)
                items = []
                for i, tt in enumerate(grp):
                    for dc in range(DC):
                        items.append((stg(i, t_hi)[:, dc * 128:(dc + 1) * 128],
                                      t_hioT[:, dc * TOK + tt * 128: dc * TOK + (tt + 1) * 128]))
                vdone[gi] = self.transp(p, items, [ps_trA, ps_trB],
                                        extra_waits=[(V, v1)])
            p.emit()
            if "hio" in self.dumpreq:
                self.emit_dump("hio", [128, DC * TOK], BF16, t_hioT[:, :], V, V.n)

            if _osA.environ.get("MK_SKIP_QPROJ"):
                self.T_setupA = self.T.n
                return
            with nc.sbuf_tensor("qwn", [128, 2 * D], BF16) as t_qwn, \
                 nc.sbuf_tensor("qwT", [128, 2 * D], BF16) as t_qwT:
                ps_q, ps_q2 = self.psf[2], self.psf[3]
                p = Plan(self)
                tprev = {}
                vev = {}
                for hp in range(HP):
                    par = hp % 2
                    w = [(T, tprev.get(hp - 2))] if hp >= 2 else []
                    wd = p.op("gpsimd", lambda g, hp=hp, par=par: g.dma_start(
                        out=t_qwn[:, par * D:(par + 1) * D],
                        in_=self.inw[hp * 128:(hp + 1) * 128, :]),
                        waits=w, inc=WD, inck=16)
                    items = [(t_qwn[:, par * D + dc * 128: par * D + (dc + 1) * 128],
                              t_qwT[:, par * D + dc * 128: par * D + (dc + 1) * 128])
                             for dc in range(DC)]
                    vq = self.transp(p, items, [ps_trA],
                                     extra_waits=[("WDRAW", wd)])
                    for qc in range(4):
                        ps = ps_q if qc % 2 == 0 else ps_q2
                        w2 = [(V, vq)]
                        if (hp, qc - 2) in vev:
                            w2.append((V, vev[(hp, qc - 2)]))

                        def qmm(t, hp=hp, par=par, qc=qc, ps=ps):
                            last = None
                            for dc in range(DC):
                                last = t.matmul(
                                    ps[:, 0:512],
                                    t_qwT[:, par * D + dc * 128: par * D + (dc + 1) * 128],
                                    t_hioT[:, dc * TOK + qc * 512: dc * TOK + (qc + 1) * 512],
                                    start=(dc == 0), stop=(dc == DC - 1))
                            return last
                        tq = p.op("tensor", qmm, waits=w2, inc=T)
                        vev[(hp, qc)] = p.op("vector", lambda v, hp=hp, qc=qc, ps=ps:
                                             v.tensor_scalar_mul(
                                self.t_Qp[:, hp * TOK + qc * 512: hp * TOK + (qc + 1) * 512],
                                ps[:, 0:512], 0.125), waits=[(T, tq)], inc=V)
                    tprev[hp] = tq
                for eng in p.ops:
                    fixed = []
                    for fn, w, inc, inck, o in p.ops[eng]:
                        w = [(self.WD if s == "WDRAW" else s, v) for (s, v) in w]
                        fixed.append((fn, w, inc, inck, o))
                    p.ops[eng] = fixed
                p.emit()
            if "qp" in self.dumpreq:
                self.emit_dump("qp", [128, HP * TOK], BF16, self.t_Qp[:, :], V, V.n)
        self.T_setupA = self.T.n

    def setupB(self):
        nc, LD, V = self.nc, self.LD, self.V
        with nc.sbuf_tensor("xstgB", [128, 4 * D], F32) as t_xstg, \
             nc.sbuf_tensor("xnstgB", [128, 4 * D], BF16) as t_xn, \
             nc.sbuf_tensor("histgB", [128, 4 * D], BF16) as t_hi, \
             nc.sbuf_tensor("sqscrB", [128, 2 * D], F32) as t_sq, \
             nc.sbuf_tensor("lncolsB", [128, 32], F32) as t_cols:
            ps_trA, ps_trB = self.ps_tr, self.ps_tr2
            p = Plan(self)
            import os as _osB
            nsb = int(_osB.environ.get("MK_NSB", NTB // 2))
            vdone = {}
            for gi in range(nsb):
                grp = [2 * gi, 2 * gi + 1]
                par = gi % 2
                stg = lambda i, t: t[:, (par * 2 + i) * D:(par * 2 + i + 1) * D]
                w = [(V, vdone.get(gi - 2))] if gi >= 2 else                     [(self.T, getattr(self, "T_setupA", 0))]

                def ldx(sy, grp=grp, par=par):
                    return [sy.dma_start(
                        out=t_xstg[:, (par * 2 + i) * D:(par * 2 + i + 1) * D],
                        in_=self.xb[tt * 128:(tt + 1) * 128, :])
                        for i, tt in enumerate(grp)]
                ldh = p.op("sync", ldx, waits=w, inc=LD, inck=32)
                tiles0 = [dict(src=stg(i, t_xstg), ready=[(LD, ldh)],
                               out=stg(i, t_xn), gb=None) for i in range(2)]
                v0 = self.ln_chain(p, tiles0, t_cols[:, 0:16], t_sq)
                tiles1 = [dict(src=stg(i, t_xn), ready=[(V, v0)],
                               out=stg(i, t_hi),
                               gb=(self.t_reps1[:, 0:D], self.t_reps1[:, D:2 * D]),
                               tmp=t_sq[:, i * D:(i + 1) * D]) for i in range(2)]
                v1 = self.ln_chain(p, tiles1, t_cols[:, 16:32], t_sq)
                items = []
                for i, tt in enumerate(grp):
                    for dc in range(DC):
                        items.append((stg(i, t_hi)[:, dc * 128:(dc + 1) * 128],
                                      self.t_hT[:, dc * L + tt * 128: dc * L + (tt + 1) * 128]))
                vdone[gi] = self.transp(p, items, [ps_trA, ps_trB],
                                        extra_waits=[(V, v1)])
            p.emit()
            self.vB_end = self.V.n
        if "hT" in self.dumpreq:
            self.emit_dump("hT", [128, DC * L], BF16, self.t_hT[:, :], V, V.n)

    def pairs(self):
        nc = self.nc
        with ExitStack() as pes:
            specs = [
                ("t_kwn", "sbuf", [128, D], BF16),
                ("t_kwT", "sbuf", [128, D], BF16),
                ("t_kcb", "sbuf", [128, 2 * 512], BF16),
                ("t_t5", "sbuf", [128, 512], BF16),
                ("t_Ka", "sbuf", [65, L], BF16),
                ("t_Kb", "sbuf", [65, L], BF16),
                ("t_Va", "sbuf", [128, 32 * 65], BF16),
                ("t_Vb", "sbuf", [128, 32 * 65], BF16),
                ("t_vtb", "sbuf", [128, 2 * 512], BF16),
                ("t_q2a", "sbuf", [65, TOK], BF16),
                ("t_q2b", "sbuf", [65, TOK], BF16),
                ("t_ptb", "sbuf", [128, 2 * 512], BF16),
                ("t_o64", "sbuf", [64, 512], BF16),
                ("t_rrow", "sbuf", [1, 512], F32),
                ("t_mc", "sbuf", [128, 8], F32),
            ]
            tn = {}
            for name, kind, shape, dt in specs:
                mk = nc.sbuf_tensor if kind == "sbuf" else nc.psum_tensor
                tn[name] = pes.enter_context(mk(name, shape, dt))
            tn["ps_pA"] = self.psf[0]
            tn["ps_pB"] = self.psf[1]
            tn["ps_2a"] = self.psf[2]
            tn["ps_2b"] = self.psf[3]
            tn["ps_o4a"] = self.psf[4]
            tn["ps_o4b"] = self.psf[5]
            tn["ps_trP"] = self.ps_tr
            t_Ka, t_Kb = tn["t_Ka"], tn["t_Kb"]
            t_Va, t_Vb = tn["t_Va"], tn["t_Vb"]

            p = Plan(self)

            def initk(v):
                v.memset(t_Ka[64:65, :], 1.0)
                v.memset(t_Kb[64:65, :], 1.0)
                last = None
                for ck in range(32):
                    v.memset(t_Va[:, ck * 65 + 64:ck * 65 + 65], 1.0)
                    last = v.memset(t_Vb[:, ck * 65 + 64:ck * 65 + 65], 1.0)
                import os as _osm
                if _osm.environ.get("MK_MEMSET_OT"):
                    for ck in range(HP * TOK // 512):
                        last = v.memset(self.t_oT[:, ck * 512:(ck + 1) * 512], 0.0)
                return last
            self.v_initk = p.op("vector", initk, inc=self.V)
            p.emit()

            self.st = dict(prevT=self.T.n, p2A={}, avT={}, o4V=dict(), lastexpA=None,
                           p2idx=0)
            import os
            npairs = int(os.environ.get("MK_NPAIRS", HP))
            for hp in range(npairs):
                self.one_pair(hp, tn)
            if "oT" in self.dumpreq and self.stage < 2:
                self.emit_dump("oT", [128, HP * TOK], BF16, self.t_oT[:, :],
                               self.V, self.V.n)

    def one_pair(self, hp, tn):
        T, V, A, WD = self.T, self.V, self.A, self.WD
        st = self.st
        p = Plan(self)
        t_kwn, t_kwT, t_kcb, t_t5 = tn["t_kwn"], tn["t_kwT"], tn["t_kcb"], tn["t_t5"]
        t_Ka, t_Kb, t_Va, t_Vb = tn["t_Ka"], tn["t_Kb"], tn["t_Va"], tn["t_Vb"]
        t_vtb, t_q2a, t_q2b = tn["t_vtb"], tn["t_q2a"], tn["t_q2b"]
        t_mc = tn["t_mc"]
        ps_pA, ps_pB = tn["ps_pA"], tn["ps_pB"]
        ps_2a, ps_2b = tn["ps_2a"], tn["ps_2b"]
        ps_trP = tn["ps_trP"]
        prevT = st["prevT"]

        wk = [(T, prevT)]
        if hp == 0:
            wk.append((V, getattr(self, "vB_end", 0)))
        wd = p.op("gpsimd", lambda g: g.dma_start(
            out=t_kwn[:, 0:D], in_=self.inw[D + hp * 128: D + (hp + 1) * 128, :]),
            waits=wk, inc=WD, inck=16)
        items = [(t_kwn[:, dc * 128:(dc + 1) * 128], t_kwT[:, dc * 128:(dc + 1) * 128])
                 for dc in range(DC)]
        ew = [(WD, wd)]
        if hp == 0:
            ew.append((V, getattr(self, "vB_end", 0)))
        import os as _osx
        if _osx.environ.get("MK_SKIP_KWT"):
            vkw = p.op("vector", lambda v: v.memset(t_kwT[0:1, 0:1], 0.0),
                       waits=ew, inc=V)
        else:
            vkw = self.transp(p, items, [ps_trP], extra_waits=ew)

        import os as _os3
        svA = self.t_sv[:, hp:hp + 1]
        svA1m = self.t_sv[:, HP + hp:HP + hp + 1]
        kev = {}
        nkproj = 0 if _os3.environ.get("MK_SKIP_KPROJ") else 8
        for cc in range(nkproj):
            ps = ps_2a if cc % 2 == 0 else ps_2b
            w = [(V, vkw)] if cc == 0 else []
            if cc >= 2:
                w.append((V, kev[cc - 2]))
            if cc < 2 and id(ps) in self._psV:
                w.append((V, self._psV[id(ps)]))

            def kmm(t, cc=cc, ps=ps):
                last = None
                for dc in range(DC):
                    last = t.matmul(ps[:, 0:512], t_kwT[:, dc * 128:(dc + 1) * 128],
                                    self.t_hT[:, dc * L + cc * 512: dc * L + (cc + 1) * 512],
                                    start=(dc == 0), stop=(dc == DC - 1))
                return last
            tk = p.op("tensor", kmm, waits=w, inc=T)

            def ksm(v, cc=cc, ps=ps):
                par = cc % 2
                v.tensor_copy(t_kcb[:, par * 512:(par + 1) * 512], ps[:, 0:512])
                v.nop(nofuse=True)
                v.tensor_scalar_mul(t_t5[:, :], t_kcb[:, par * 512:(par + 1) * 512], svA1m)
                v.nop(nofuse=True)
                c0 = cc * 512
                for t_K, lo in ((t_Ka, 0), (t_Kb, 64)):
                    v.scalar_tensor_tensor(
                        t_K[0:64, c0 + 1:c0 + 512],
                        t_kcb[lo:lo + 64, par * 512:par * 512 + 511],
                        svA[lo:lo + 64, :],
                        t_t5[lo:lo + 64, 1:512], ALU.mult, ALU.add)
                if cc == 0:
                    v.tensor_copy(t_Ka[0:64, 0:1], t_t5[0:64, 0:1])
                    last = v.tensor_copy(t_Kb[0:64, 0:1], t_t5[64:128, 0:1])
                else:
                    pl = 1 - par
                    v.scalar_tensor_tensor(t_Ka[0:64, c0:c0 + 1],
                                           t_kcb[0:64, pl * 512 + 511:pl * 512 + 512],
                                           svA[0:64, :], t_t5[0:64, 0:1],
                                           ALU.mult, ALU.add)
                    last = v.scalar_tensor_tensor(t_Kb[0:64, c0:c0 + 1],
                                                  t_kcb[64:128, pl * 512 + 511:pl * 512 + 512],
                                                  svA[64:128, :], t_t5[64:128, 0:1],
                                                  ALU.mult, ALU.add)
                return last
            kev[cc] = p.op("vector", ksm, waits=[(T, tk)], inc=V)
            self._psV[id(ps)] = kev[cc]
        v_smear = kev.get(7, vkw)
        kev.setdefault(6, vkw)
        kev.setdefault(7, vkw)
        tk_kproj = tk if nkproj else self.T.n

        if _osx.environ.get("MK_SKIP_WD2"):
            vvw = vkw
        else:
            wd2 = p.op("gpsimd", lambda g: g.dma_start(
                out=t_kwn[:, 0:D], in_=self.inw[2 * D + hp * 128: 2 * D + (hp + 1) * 128, :]),
                waits=[(V, vkw)], inc=WD, inck=16)
            items = [(t_kwn[:, dc * 128:(dc + 1) * 128], t_kwT[:, dc * 128:(dc + 1) * 128])
                     for dc in range(DC)]
            vvw = self.transp(p, items, [ps_trP],
                              extra_waits=[(WD, wd2), (T, tk_kproj)])

        def q2cp(v):
            last = None
            for qc in range(4):
                v.tensor_copy(t_q2a[0:64, qc * 512:(qc + 1) * 512],
                              self.t_Qp[0:64, hp * TOK + qc * 512: hp * TOK + (qc + 1) * 512])
                last = v.tensor_copy(t_q2b[0:64, qc * 512:(qc + 1) * 512],
                                     self.t_Qp[64:128, hp * TOK + qc * 512: hp * TOK + (qc + 1) * 512])
            return last
        if _osx.environ.get("MK_SKIP_Q2CP"):
            v_q2c = vkw
        else:
            v_q2c = p.op("vector", q2cp, waits=[(T, prevT)], inc=V)

        import os as _os2
        skip_p1 = bool(_os2.environ.get("MK_SKIP_P1"))
        p1red = {}
        vev = {}
        vtev = {}
        for s in range(NS):
            Wd = min(512, 256 * (s + 1))
            w0 = 256 * (s + 1) - Wd
            w = [(V, v_smear), (V, v_q2c)] if s == 0 else [(V, p1red[s - 1])]
            if skip_p1:
                p1red[s] = p1red.get(s - 1, v_q2c)
                if _os2.environ.get("MK_SKIP_VPROJ"):
                    continue
                if s % 2 == 0:
                    cc = s // 2
                    ps = ps_2a if cc % 2 == 0 else ps_2b
                    wv = [(V, kev[6 + (cc % 2)]), (V, vvw)] if cc < 2 else [(V, vev[cc - 2])]
                    tv = p.op("tensor", lambda t, cc=cc, ps=ps: [
                        t.matmul(ps[:, 0:512], t_kwT[:, dc * 128:(dc + 1) * 128],
                                 self.t_hT[:, dc * L + cc * 512: dc * L + (cc + 1) * 512],
                                 start=(dc == 0), stop=(dc == DC - 1))
                        for dc in range(DC)][-1], waits=wv, inc=T)
                    vev[cc] = p.op("vector", lambda v, cc=cc, ps=ps: v.tensor_copy(
                        t_vtb[:, (cc % 2) * 512:(cc % 2 + 1) * 512], ps[:, 0:512]),
                        waits=[(T, tv)], inc=V)
                    self._psV[id(ps)] = vev[cc]
                    if cc >= 1:
                        vtev[cc - 1] = self.emit_vtransp(p, cc - 1, tn, vev[cc - 1],
                                                         vtev.get(cc - 2))
                continue
            tp1 = p.op("tensor", lambda t, s=s, Wd=Wd, w0=w0: (
                t.matmul(ps_pA[:, 0:Wd],
                         t_q2a[0:64, s * 128:(s + 1) * 128],
                         t_Ka[0:64, w0:w0 + Wd], start=True, stop=True),
                t.matmul(ps_pB[:, 0:Wd],
                         t_q2b[0:64, s * 128:(s + 1) * 128],
                         t_Kb[0:64, w0:w0 + Wd], start=True, stop=True))[-1],
                waits=w, inc=T)

            if s % 2 == 0:
                cc = s // 2
                ps = ps_2a if cc % 2 == 0 else ps_2b
                wv = [(V, kev[6 + (cc % 2)]), (V, vvw)] if cc < 2 else [(V, vev[cc - 2])]
                tv = p.op("tensor", lambda t, cc=cc, ps=ps: [
                    t.matmul(ps[:, 0:512], t_kwT[:, dc * 128:(dc + 1) * 128],
                             self.t_hT[:, dc * L + cc * 512: dc * L + (cc + 1) * 512],
                             start=(dc == 0), stop=(dc == DC - 1))
                    for dc in range(DC)][-1], waits=wv, inc=T)
                vev[cc] = p.op("vector", lambda v, cc=cc, ps=ps: v.tensor_copy(
                    t_vtb[:, (cc % 2) * 512:(cc % 2 + 1) * 512], ps[:, 0:512]),
                    waits=[(T, tv)], inc=V)
                self._psV[id(ps)] = vev[cc]
                if cc >= 1:
                    vtev[cc - 1] = self.emit_vtransp(p, cc - 1, tn, vev[cc - 1],
                                                     vtev.get(cc - 2))

            vpre = p.op("vector", lambda v, s=s, Wd=Wd: (
                v.reduce_max(t_mc[:, 0:1], ps_pA[:, 0:Wd], axis=AX.X, negate=True),
                v.reduce_max(t_mc[:, 4:5], ps_pB[:, 0:Wd], axis=AX.X, negate=True),
                v.nop(nofuse=True),
                v.scalar_tensor_tensor(t_mc[:, 1:2], self.t_qpos[:, s:s + 1],
                                       self.t_srep[:, H + 2 * hp:H + 2 * hp + 1],
                                       t_mc[:, 0:1], ALU.mult, ALU.add),
                v.scalar_tensor_tensor(t_mc[:, 5:6], self.t_qpos[:, s:s + 1],
                                       self.t_srep[:, H + 2 * hp + 1:H + 2 * hp + 2],
                                       t_mc[:, 4:5], ALU.mult, ALU.add))[-1],
                waits=[(T, tp1)], inc=V)
            tng = p.op("tensor", lambda t: (
                t.transpose(ps_pA[0:1, 0:128], t_mc[:, 1:2], self.t_identf[:, :]),
                t.transpose(ps_pB[0:1, 0:128], t_mc[:, 5:6], self.t_identf[:, :]))[-1],
                waits=[(V, vpre)], inc=T)
            p1red[s] = p.op("vector", lambda v, s=s: (
                v.tensor_copy(t_q2a[64:65, s * 128:(s + 1) * 128], ps_pA[0:1, 0:128]),
                v.tensor_copy(t_q2b[64:65, s * 128:(s + 1) * 128], ps_pB[0:1, 0:128]))[-1],
                waits=[(T, tng)], inc=V)
        if not (skip_p1 and _os2.environ.get("MK_SKIP_VPROJ")):
            vtev[7] = self.emit_vtransp(p, 7, tn, vev[7], vtev.get(6))
        v_q2done = p1red[NS - 1]

        import os as _os
        if _os.environ.get("MK_SKIP_HEADS"):
            st["prevT"] = self.T.n
            p.emit()
            return
        for hd in range(2):
            h = 2 * hp + hd
            wv = [(V, v_q2done), (V, vtev[7])]
            if st["lastexpA"] is not None:
                wv.append((A, st["lastexpA"]))
            v_alibi = p.op("vector", lambda v, h=h: v.tensor_scalar_mul(
                self.t_alibi[:, :], self.t_jpos[:, :], self.t_srep[:, h:h + 1]),
                waits=wv, inc=V)
            self.attn_head(p, hp, hd, tn, v_alibi)
        st["prevT"] = self.T.n
        p.emit()

    def emit_vtransp(self, p, cc, tn, vev_cc, vt_prev):
        T, V = self.T, self.V
        t_vtb, t_Va, t_Vb, ps_trP = tn["t_vtb"], tn["t_Va"], tn["t_Vb"], tn["ps_trP"]
        par = cc % 2
        w = [(V, vev_cc)]
        if id(ps_trP) in self._psV:
            w.append((V, self._psV[id(ps_trP)]))
        tk = p.op("tensor", lambda t, cc=cc, par=par: [
            t.transpose(ps_trP[:, i * 128:(i + 1) * 128],
                        t_vtb[:, par * 512 + i * 128: par * 512 + (i + 1) * 128],
                        self.t_identbf[:, :]) for i in range(4)][-1],
            waits=w, inc=T)

        def vf(v, cc=cc):
            last = None
            for i in range(4):
                ck = cc * 4 + i
                v.tensor_copy(t_Va[:, ck * 65: ck * 65 + 64], ps_trP[:, i * 128:i * 128 + 64])
                last = v.tensor_copy(t_Vb[:, ck * 65: ck * 65 + 64],
                                     ps_trP[:, i * 128 + 64:(i + 1) * 128])
            return last
        out = p.op("vector", vf, waits=[(T, tk)], inc=V)
        self._psV[id(ps_trP)] = out
        return out

    def attn_head(self, p, hp, hd, tn, v_alibi):
        T, V, A = self.T, self.V, self.A
        st = self.st
        t_K = tn["t_Ka"] if hd == 0 else tn["t_Kb"]
        t_Vp = tn["t_Va"] if hd == 0 else tn["t_Vb"]
        t_q2 = tn["t_q2a"] if hd == 0 else tn["t_q2b"]
        t_ptb, t_o64, t_rrow = tn["t_ptb"], tn["t_o64"], tn["t_rrow"]
        ps2 = [tn["ps_2a"], tn["ps_2b"]]
        pso = [tn["ps_o4a"], tn["ps_o4b"]]
        psr = [tn["ps_pA"], tn["ps_pB"]]
        h = 2 * hp + hd

        for grp in range(2):
            for c in range(2 * (8 * grp + 7) + 2):
                smin = max(c // 2, 8 * grp)
                sgmin = smin - 8 * grp
                for bt in range(2):
                    b0 = bt * 4
                    if sgmin >= b0 + 4:
                        continue
                    lo = max(sgmin - b0, 0) * 128
                    slots0 = 8 * grp + b0
                    i = st["p2idx"]
                    st["p2idx"] += 1
                    par = i % 2
                    ps = ps2[par]
                    w = [(V, v_alibi)]
                    if par in st["p2A"]:
                        w.append((A, st["p2A"][par]))
                    if id(ps) in self._psV:
                        w.append((V, self._psV.pop(id(ps))))
                    tmm = p.op("tensor", lambda t, c=c, lo=lo, ps=ps, slots0=slots0:
                               t.matmul(ps[:, lo:512], t_K[0:65, c * 128:(c + 1) * 128],
                                        t_q2[0:65, slots0 * 128 + lo: (slots0 + 4) * 128],
                                        start=True, stop=True),
                               waits=w, inc=T)
                    vmask = None
                    if c // 2 == smin and b0 <= sgmin < b0 + 4:
                        moff = 0 if c % 2 == 0 else 128
                        col = (sgmin - b0) * 128
                        vmask = p.op("vector", lambda v, ps=ps, col=col, moff=moff:
                                     v.tensor_add(ps[:, col:col + 128], ps[:, col:col + 128],
                                                  self.t_masks[:, moff:moff + 128]),
                                     waits=[(T, tmm)], inc=V)
                    we = [(V, vmask)] if vmask is not None else [(T, tmm)]
                    if par in st["avT"]:
                        we.append((T, st["avT"][par]))
                    aex = p.op("scalar", lambda sc, c=c, lo=lo, ps=ps, par=par:
                               sc.activation(t_ptb[:, par * 512 + lo: (par + 1) * 512],
                                             ps[:, lo:512], AF.Exp,
                                             bias=self.t_alibi[:, c:c + 1]),
                               waits=we, inc=A)
                    st["p2A"][par] = aex
                    st["lastexpA"] = aex
                    wa = [(A, aex)]
                    first = (c == 0)
                    last = (c == 2 * (slots0 + 3) + 1)
                    if first and (hp, hd, grp, bt) != (0, 0, 0, bt) and bt in st["o4V"]:
                        wa.append((V, st["o4V"][bt]))
                    st["avT"][par] = p.op(
                        "tensor", lambda t, c=c, lo=lo, par=par, bt=bt, first=first,
                        last=last: t.matmul(pso[bt][0:65, lo:512],
                                            t_Vp[:, c * 65:(c + 1) * 65],
                                            t_ptb[:, par * 512 + lo:(par + 1) * 512],
                                            start=first, stop=last),
                        waits=wa, inc=T)
                    if last:
                        vv = p.op("vector", lambda v, bt=bt: (
                            v.tensor_copy(t_o64[:, :], pso[bt][0:64, 0:512]),
                            v.reciprocal(t_rrow[:, :], pso[bt][64:65, 0:512]))[-1],
                            waits=[(T, st["avT"][par])], inc=V)
                        tr = p.op("tensor", lambda t, bt=bt: t.matmul(
                            psr[bt][0:64, 0:512], self.t_onesf[0:1, 0:64],
                            t_rrow[0:1, 0:512], start=True, stop=True),
                            waits=[(V, vv)], inc=T)
                        row0 = hd * 64
                        colb = hp * TOK + (grp * 8 + bt * 4) * 128
                        st["o4V"][bt] = p.op("vector", lambda v, bt=bt, row0=row0,
                                             colb=colb: v.tensor_mul(
                            self.t_oT[row0:row0 + 64, colb:colb + 512],
                            t_o64[:, :], psr[bt][0:64, 0:512]),
                            waits=[(T, tr)], inc=V)

    def outproj(self):
        nc, WD, T, V = self.nc, self.WD, self.T, self.V
        v_ow = nc.virtual_sbuf_tensor("ownow", [128, HP * TOK], BF16)
        nc.back_virtual_tensor_with(v_ow, self.t_ar2)
        t_own = OffV(v_ow, 0, 8 * D)
        t_owT = OffV(v_ow, 8 * D, 8 * D)
        with nc.sbuf_tensor("spreb", [128, 2 * D], F32) as t_spre, \
             nc.sbuf_tensor("sqscrO", [128, 2 * D], F32) as t_sq, \
             nc.sbuf_tensor("lncolsO", [128, 32], F32) as t_cols, \
             nc.sbuf_tensor("lnrow2", [1, 2 * D], F32) as t_lnr2, \
             nc.sbuf_tensor("reps2", [128, 2 * D], F32) as t_reps2:
            ps_trO = self.ps_tr
            psl = self.psf[0:4]
            p = Plan(self)
            LD = self.LD
            vpairs_end = self.V.n
            lastA_pairs = self.st["lastexpA"] or 0
            ldr = p.op("sync", lambda sy: [
                sy.dma_start(out=t_lnr2[:, 0:D], in_=self.ln2g_d),
                sy.dma_start(out=t_lnr2[:, D:2 * D], in_=self.ln2b_d)],
                waits=[(self.T, self.st["prevT"]), (self.V, vpairs_end)],
                inc=LD, inck=32)
            rp = {}
            for k in range(4):
                ps = psl[k]
                w = [(LD, ldr), (T, self.st["prevT"]),
                     (self.V, vpairs_end), (self.A, lastA_pairs)]
                tk = p.op("tensor", lambda t, k=k, ps=ps: t.matmul(
                    ps[:, 0:512], self.t_onesf[0:1, :],
                    t_lnr2[0:1, k * 512:(k + 1) * 512],
                    start=True, stop=True), waits=w, inc=T)
                rp[k] = p.op("vector", lambda v, k=k, ps=ps: v.tensor_copy(
                    t_reps2[:, k * 512:(k + 1) * 512], ps[:, 0:512]),
                    waits=[(T, tk)], inc=V)
            v_reps2 = rp[3]
            wd = p.op("gpsimd", lambda g: [
                g.dma_start(out=t_own[:, r * D:(r + 1) * D],
                            in_=self.outw[r * 128:(r + 1) * 128, :])
                for r in range(DC)],
                waits=[(T, self.st["prevT"]), (self.V, vpairs_end)],
                inc=WD, inck=128)
            items = []
            for r in range(DC):
                for oc in range(DC):
                    items.append((t_own[:, r * D + oc * 128: r * D + (oc + 1) * 128],
                                  t_owT[:, oc * D + r * 128: oc * D + (r + 1) * 128]))
            ow_extra = [(WD, wd), (self.V, vpairs_end), (self.A, lastA_pairs)]
            import os as _o7
            nd = int(_o7.environ.get("MK_OWDELAY", 0))
            if nd:
                vdel = p.op("vector", lambda v: [
                    v.memset(t_cols[:, 0:1], 0.0) for _ in range(nd)][-1],
                    waits=[(WD, wd)], inc=V)
                ow_extra.append((V, vdel))
            if _o7.environ.get("MK_OWDBG"):
                vow = self.transp(p, items[:8], [ps_trO], extra_waits=ow_extra)
                vcp = p.op("vector", lambda v: v.tensor_copy(
                    t_sq[:, 0:1024], ps_trO[:, :]), waits=[(V, vow)], inc=V)
                p.emit()
                self.v_s = self.V.n
                self.emit_dump("psdbg", [128, 1024], F32, t_sq[:, 0:1024],
                               self.V, self.V.n)
                self.emit_dump("ow", [128, 8 * D], BF16, t_owT[:, :],
                               self.V, self.V.n)
                self.emit_dump("own", [128, 8 * D], BF16, t_own[:, :],
                               self.V, self.V.n)
                self.emit_dump("idb", [128, 128], BF16, self.t_identbf[:, :],
                               self.V, self.V.n)
                return
            vow = self.transp(p, items, [ps_trO], extra_waits=ow_extra)
            vln = {}
            evv = {}
            for tt in range(NT):
                par = tt % 2
                tmm = {}
                for nh in range(2):
                    ps = psl[par * 2 + nh]
                    w = [(V, vow), (V, v_reps2)]
                    if tt >= 2:
                        w.append((V, evv[tt - 2]))
                    elif True:
                        w.append((V, rp[par * 2 + nh]))

                    import os as _o6
                    if _o6.environ.get("MK_SPMM_IDID"):
                        def spmm(t, tt=tt, nh=nh, ps=ps):
                            return t.matmul(
                                ps[:, 0:128], self.t_identbf[:, :],
                                self.t_identbf[:, :],
                                start=True, stop=True)
                    elif _o6.environ.get("MK_SPMM_IDENT"):
                        def spmm(t, tt=tt, nh=nh, ps=ps):
                            return t.matmul(
                                ps[:, 0:512], self.t_identbf[:, :],
                                t_owT[:, nh * 512:(nh + 1) * 512],
                                start=True, stop=True)
                    elif _o6.environ.get("MK_SPMM1"):
                        def spmm(t, tt=tt, nh=nh, ps=ps):
                            return t.matmul(
                                ps[:, 0:512],
                                self.t_oT[:, tt * 128:(tt + 1) * 128],
                                t_owT[:, nh * 512:(nh + 1) * 512],
                                start=True, stop=True)
                    else:
                        def spmm(t, tt=tt, nh=nh, ps=ps):
                            last = None
                            for oc in range(DC):
                                last = t.matmul(
                                    ps[:, 0:512],
                                    self.t_oT[:, oc * TOK + tt * 128: oc * TOK + (tt + 1) * 128],
                                    t_owT[:, oc * D + nh * 512: oc * D + (nh + 1) * 512],
                                    start=(oc == 0), stop=(oc == DC - 1))
                            return last
                    tmm[nh] = p.op("tensor", spmm, waits=w, inc=T)
                w = [(T, tmm[0]), (T, tmm[1])]
                if tt >= 2:
                    w.append((V, vln[tt - 2]))
                evv[tt] = p.op("vector", lambda v, tt=tt, par=par: (
                    v.tensor_copy(t_spre[:, par * D:par * D + 512],
                                  psl[par * 2][:, 0:512]),
                    v.tensor_copy(t_spre[:, par * D + 512:par * D + 1024],
                                  psl[par * 2 + 1][:, 0:512]))[-1],
                    waits=w, inc=V)
                tiles = [dict(src=t_spre[:, par * D:(par + 1) * D],
                              ready=[(V, evv[tt])],
                              out=self.t_s[:, tt * D:(tt + 1) * D],
                              gb=(t_reps2[:, 0:D], t_reps2[:, D:2 * D]),
                              tmp=t_sq[:, par * D:(par + 1) * D])]
                vln[tt] = self.ln_chain(p, tiles, t_cols[:, par * 16:par * 16 + 16],
                                        t_sq[:, par * D:(par + 1) * D])
            p.emit()
            self.v_s = self.V.n
            if "s" in self.dumpreq:
                self.emit_dump("s", [128, NT * D], BF16, self.t_s[:, 0:NT * D],
                               self.V, self.V.n)
            if "ow" in self.dumpreq:
                self.emit_dump("ow", [128, 8 * D], BF16, t_owT[:, :], self.V, self.V.n)
            if "own" in self.dumpreq:
                self.emit_dump("own", [128, 8 * D], BF16, t_own[:, :], self.V, self.V.n)
            if "spre" in self.dumpreq:
                self.emit_dump("spre", [128, 2 * D], F32, t_spre[:, :], self.V, self.V.n)
        if "oT" in self.dumpreq:
            self.emit_dump("oT", [128, HP * TOK], BF16, self.t_oT[:, :], self.V, self.V.n)

    def moe(self):
        nc, LD, WD, T, V, A = self.nc, self.LD, self.WD, self.T, self.V, self.A
        with ExitStack() as mes:
            specs = [
                ("lgb", "sbuf", [128, 48], F32),
                ("b1T", "sbuf", [128, 32 * NE], F32),
                ("b2s", "sbuf", [NE, D], BF16),
                ("rws", "sbuf", [128, DC * NE], BF16),
                ("rbs", "sbuf", [1, NE], BF16),
                ("gT", "sbuf", [NE, TOK], BF16),
            ]
            tns = {}
            for name, kind, shape, dt in specs:
                mk = nc.sbuf_tensor if kind == "sbuf" else nc.psum_tensor
                tns[name] = mes.enter_context(mk(name, shape, dt))
            t_xnT = nc.virtual_sbuf_tensor("xnT", [128, DC * TOK], BF16)
            nc.back_virtual_tensor_with(t_xnT, self.t_ar2)
            t_y = OffV(self.t_s, NT * D, NT * D)
            t_lgb = tns["lgb"]
            ps_h0, ps_h1 = self.psf[0], self.psf[1]
            psy = [self.psf[2], self.psf[3], self.psf[4], self.psf[5]]
            ps_tg = self.psf[0]
            ps_tr2 = self.ps_tr
            t_b1T, t_b2 = tns["b1T"], tns["b2s"]
            t_rw, t_rb, t_gT = tns["rws"], tns["rbs"], tns["gT"]
            self.t_b1T, self.t_b2, self.t_rw, self.t_rb, self.t_gT = \
                t_b1T, t_b2, t_rw, t_rb, t_gT
            s1 = ExitStack()
            t_b1n = s1.enter_context(nc.sbuf_tensor("b1n", [NE, DFF], F32))
            t_xstg = s1.enter_context(nc.sbuf_tensor("xstgM", [128, 4 * D], F32))
            t_xn = s1.enter_context(nc.sbuf_tensor("xnstgM", [128, 4 * D], BF16))
            t_sqM = s1.enter_context(nc.sbuf_tensor("sqscrM", [128, 2 * D], F32))
            t_colsM = s1.enter_context(nc.sbuf_tensor("lncolsM", [128, 32], F32))

            p = Plan(self)
            v_attn_done = getattr(self, "v_s", 0)
            ldb = p.op("sync", lambda sy: sy.dma_start(out=t_b1n[:, :], in_=self.b1_d),
                       waits=[(V, v_attn_done)], inc=LD, inck=16)
            wdb = p.op("gpsimd", lambda g: [
                g.dma_start(out=t_b2[:, :], in_=self.b2_d),
                g.dma_start(out=t_rw[:, :].rearrange("p (c e) -> p c e", c=DC),
                            in_=self.rw_d.rearrange("(c p) e -> p c e", p=128)),
                g.dma_start(out=t_rb[:, :], in_=self.rb_d)],
                waits=[(V, v_attn_done)], inc=WD, inck=48)
            self.wd_moe = wdb
            prevb = {}
            for c in range(32):
                ps = ps_h0 if c % 2 == 0 else ps_h1
                w = [(LD, ldb)]
                if c < 2:
                    w.append((V, v_attn_done))
                if c >= 2:
                    w.append((V, prevb[c - 2]))
                tk = p.op("tensor", lambda t, c=c, ps=ps: t.transpose(
                    ps[:, 0:NE], t_b1n[0:NE, c * 128:(c + 1) * 128],
                    self.t_identf[0:NE, 0:NE]), waits=w, inc=T)
                prevb[c] = p.op("vector", lambda v, c=c, ps=ps: v.tensor_copy(
                    t_b1T[:, c * NE:(c + 1) * NE], ps[:, 0:NE]),
                    waits=[(T, tk)], inc=V)
            self.v_b1T = prevb[31]
            p.emit()

            p = Plan(self)
            vdone = {}
            for gi in range(NT // 2):
                grp = [2 * gi, 2 * gi + 1]
                par = gi % 2
                stg = lambda i, t: t[:, (par * 2 + i) * D:(par * 2 + i + 1) * D]
                w = [(V, vdone.get(gi - 2))] if gi >= 2 else [(V, v_attn_done)]

                def ldx(sy, grp=grp, par=par):
                    return [sy.dma_start(
                        out=t_xstg[:, (par * 2 + i) * D:(par * 2 + i + 1) * D],
                        in_=self.xo[tt * 128:(tt + 1) * 128, :])
                        for i, tt in enumerate(grp)]
                ldh = p.op("sync", ldx, waits=w, inc=LD, inck=32)
                tiles0 = [dict(src=stg(i, t_xstg), ready=[(LD, ldh)],
                               out=stg(i, t_xn), gb=None) for i in range(2)]
                v0 = self.ln_chain(p, tiles0, t_colsM[:, 0:16], t_sqM)
                items = []
                for i, tt in enumerate(grp):
                    for dc in range(DC):
                        items.append((stg(i, t_xn)[:, dc * 128:(dc + 1) * 128],
                                      t_xnT[:, dc * TOK + tt * 128: dc * TOK + (tt + 1) * 128]))
                vdone[gi] = self.transp(p, items, [ps_tr2], extra_waits=[(V, v0)])
            v_xnT = vdone[NT // 2 - 1]
            p.emit()
            if "xnT" in self.dumpreq:
                self.emit_dump("xnT", [128, DC * TOK], BF16, t_xnT[:, :], V, V.n)

            p = Plan(self)
            vg = {}
            for tt in range(NT):
                w = [(V, v_xnT), (WD, self.wd_moe), (V, self.v_b1T)]
                if tt >= 1:
                    w.append((V, vg[tt - 1]))

                def gmm(t, tt=tt):
                    for dc in range(DC):
                        t.matmul(ps_tg[:, 0:NE],
                                 t_xnT[:, dc * TOK + tt * 128: dc * TOK + (tt + 1) * 128],
                                 self.t_rw[:, dc * NE:(dc + 1) * NE],
                                 start=(dc == 0), stop=False)
                    return t.matmul(ps_tg[:, 0:NE], self.t_onesbf[0:1, :],
                                    self.t_rb[0:1, :], start=False, stop=True)
                tg = p.op("tensor", gmm, waits=w, inc=T)

                def gv(v, tt=tt):
                    c = lambda k: t_lgb[:, 40 + k:41 + k]
                    lg, e1 = t_lgb[:, 0:8], t_lgb[:, 8:16]
                    l2 = t_lgb[:, 16:24]
                    v.tensor_copy(lg, ps_tg[:, 0:NE])
                    for _ in range(4):
                        v.nop(nofuse=True)
                    v.reduce_max(c(0), lg, axis=AX.X)
                    for _ in range(4):
                        v.nop(nofuse=True)
                    v.tensor_scalar(e1, lg, c(0), None, ALU.is_equal)
                    for _ in range(4):
                        v.nop(nofuse=True)
                    v.scalar_tensor_tensor(l2, e1, -1.0e30, lg, ALU.mult, ALU.add)
                    for _ in range(4):
                        v.nop(nofuse=True)
                    v.reduce_max(c(1), l2, axis=AX.X)
                    for _ in range(4):
                        v.nop(nofuse=True)
                    return v.tensor_sub(c(2), c(0), c(1))
                vgv = p.op("vector", gv, waits=[(T, tg)], inc=V)

                def ga(sc, tt=tt):
                    c = lambda k: t_lgb[:, 40 + k:41 + k]
                    sc.activation(c(3), c(2), AF.Sigmoid)
                    return sc.activation(c(4), c(2), AF.Sigmoid, scale=-1.0)
                ag = p.op("scalar", ga, waits=[(V, vgv)], inc=A)

                def gv2(v, tt=tt):
                    c = lambda k: t_lgb[:, 40 + k:41 + k]
                    lg, e1 = t_lgb[:, 0:8], t_lgb[:, 8:16]
                    e2, ta_ = t_lgb[:, 24:32], t_lgb[:, 32:40]
                    v.tensor_scalar(e2, lg, c(1), None, ALU.is_equal)
                    v.tensor_scalar_mul(ta_, e1, c(3))
                    for _ in range(4):
                        v.nop(nofuse=True)
                    v.tensor_scalar_mul(e2, e2, c(4))
                    for _ in range(4):
                        v.nop(nofuse=True)
                    return v.tensor_add(self.t_G[:, tt * NE:(tt + 1) * NE], ta_, e2)
                vg2 = p.op("vector", gv2, waits=[(A, ag)], inc=V)
                tgt = p.op("tensor", lambda t, tt=tt: t.transpose(
                    ps_tg[0:NE, 128:256], self.t_G[:, tt * NE:(tt + 1) * NE],
                    self.t_identf[:, :]), waits=[(V, vg2)], inc=T)
                vg[tt] = p.op("vector", lambda v, tt=tt: v.tensor_copy(
                    self.t_gT[0:NE, tt * 128:(tt + 1) * 128], ps_tg[0:NE, 128:256]),
                    waits=[(T, tgt)], inc=V)
            v_gates = vg[NT - 1]

            vy0 = {}
            for tt in range(NT):
                par = tt % 2
                w = [(V, v_gates)]
                if tt >= 2:
                    w.append((V, vy0[tt - 2]))
                ty = p.op("tensor", lambda t, tt=tt, par=par: [
                    t.matmul(psy[par * 2 + nh][:, 0:512],
                             self.t_gT[0:NE, tt * 128:(tt + 1) * 128],
                             self.t_b2[0:NE, nh * 512:(nh + 1) * 512],
                             start=True, stop=True) for nh in range(2)][-1],
                    waits=w, inc=T)
                vy0[tt] = p.op("vector", lambda v, tt=tt, par=par: (
                    v.tensor_copy(t_y[:, tt * D:tt * D + 512], psy[par * 2][:, 0:512]),
                    v.tensor_copy(t_y[:, tt * D + 512:(tt + 1) * D],
                                  psy[par * 2 + 1][:, 0:512]))[-1],
                    waits=[(T, ty)], inc=V)
            v_yinit = vy0[NT - 1]
            p.emit()
            if "G" in self.dumpreq:
                self.emit_dump("G", [128, NT * NE], F32, self.t_G[:, :], V, V.n)

            s1.close()
            self._s1done = (self.T.n, self.V.n, self.A.n)
            self.v_y = v_yinit
            if self.stage >= 4:
                s2 = ExitStack()
                w1b0 = s2.enter_context(nc.sbuf_tensor("w1b0", [128, DC * FB], BF16))
                w1b1 = s2.enter_context(nc.sbuf_tensor("w1b1", [128, DC * FB], BF16))
                w2b0 = s2.enter_context(nc.sbuf_tensor("w2b0", [128, NE * D], BF16))
                w2b1 = s2.enter_context(nc.sbuf_tensor("w2b1", [128, NE * D], BF16))
                hb0 = s2.enter_context(nc.sbuf_tensor("hb0", [128, 8 * TCH], BF16))
                hb1 = s2.enter_context(nc.sbuf_tensor("hb1", [128, 8 * TCH], BF16))
                t_t1 = s2.enter_context(nc.sbuf_tensor("t1b", [128, 2 * D], BF16))
                w1b = [w1b0, w1b1]
                w2b = [w2b0, w2b1]
                hb = [hb0, hb1]
                psh = [ps_h0, ps_h1]
                st = dict(gelA={}, mm1T={}, mm2T={}, t1A={}, yV={}, wdk={},
                          stepT={})
                steps = [(e, fb) for e in range(NE) for fb in range(NFB)]
                p = Plan(self)
                for k, (e, fb) in enumerate(steps):
                    kpar = k % 2
                    if k == 0:
                        self.moe_wload(p, k, steps, w1b[0], w2b[0], None, st)
                    if k + 1 < len(steps):
                        self.moe_wload(p, k + 1, steps, w1b[1 - kpar], w2b[1 - kpar],
                                       st["stepT"].get(k - 1), st)
                    self.moe_step(p, k, e, fb, w1b[kpar], w2b[kpar], hb, psh,
                                  psy, t_t1, t_xnT, t_y, st, v_yinit)
                    st["stepT"][k] = self.T.n
                p.emit()
                self.v_y = self.V.n
                if "y" in self.dumpreq:
                    self.emit_dump("y", [128, NT * D], BF16, t_y[:, :], V, V.n)

            if self.stage >= 4:
                s2.close()
            if self.stage >= 5:
                with nc.sbuf_tensor("gsil", [128, 2 * D], BF16) as t_g, \
                     nc.sbuf_tensor("t2f", [128, 2 * D], F32) as t_t2, \
                     nc.sbuf_tensor("xfin", [128, 4 * D], F32) as t_xf, \
                     nc.sbuf_tensor("ofin", [128, 2 * D], F32) as t_of:
                    p = Plan(self)
                    vfin = {}
                    ofin = {}
                    vy = self.v_y
                    for tt in range(NT):
                        par = tt % 2
                        w = [(V, vfin.get(tt - 4))] if tt >= 4 else [(V, vy)]
                        ldf = p.op("sync", lambda sy, tt=tt: sy.dma_start(
                            out=t_xf[:, (tt % 4) * D:(tt % 4 + 1) * D],
                            in_=self.xo[tt * 128:(tt + 1) * 128, :]),
                            waits=w, inc=self.LD, inck=16)
                        wsc = [(V, vy)]
                        if tt >= 2:
                            wsc.append((V, vfin[tt - 2]))
                        af = p.op("scalar", lambda sc, tt=tt, par=par: sc.activation(
                            t_g[:, par * D:(par + 1) * D],
                            t_y[:, tt * D:(tt + 1) * D], AF.Silu),
                            waits=wsc, inc=A)
                        wv = [(A, af), (self.LD, ldf)]
                        if tt >= 2:
                            wv.append((self.O, ofin[tt - 2]))
                        vfin[tt] = p.op("vector", lambda v, tt=tt, par=par: (
                            v.tensor_mul(t_t2[:, par * D:(par + 1) * D],
                                         self.t_s[:, tt * D:(tt + 1) * D],
                                         t_g[:, par * D:(par + 1) * D]),
                            v.nop(nofuse=True),
                            v.tensor_add(t_of[:, par * D:(par + 1) * D],
                                         t_t2[:, par * D:(par + 1) * D],
                                         t_xf[:, (tt % 4) * D:(tt % 4 + 1) * D]))[-1],
                            waits=wv, inc=V)
                        ofin[tt] = p.op("sync", lambda sy, tt=tt, par=par:
                                        sy.dma_start(
                            out=self.out_d[tt * 128:(tt + 1) * 128, :],
                            in_=t_of[:, par * D:(par + 1) * D]),
                            waits=[(V, vfin[tt])], inc=self.O, inck=16)
                    p.emit()

    def moe_wload(self, p, k, steps, w1b, w2b, gateT, st):
        e, fb = steps[k]
        w = [(self.T, gateT)] if gateT is not None else []
        if k >= 1:
            w.append((self.WD, st["wdk"][k - 1]))
        if k <= 1:
            tn_, vn_, an_ = self._s1done
            w += [(self.T, tn_), (self.V, vn_), (self.A, an_)]
        st["wdk"][k] = p.op("gpsimd", lambda g, e=e, fb=fb: [
            g.dma_start(
                out=w1b[:, :].rearrange("p (c f) -> p c f", c=DC),
                in_=self.w1_d.rearrange("e (c p) (b f) -> e b p c f", p=128, f=FB)[e, fb]),
            g.dma_start(
                out=w2b[:, :].rearrange("p (c d) -> p c d", c=NE),
                in_=self.w2_d.rearrange("e (b c p) d -> e b p c d", p=128, c=NE)[e, fb])],
            waits=w, inc=self.WD, inck=32)

    def moe_step(self, p, k, e, fb, w1b, w2b, hb, psh, psy, t_t1, t_xnT, t_y,
                 st, v_yinit):
        T, V, A, WD = self.T, self.V, self.A, self.WD
        for tc in range(NTC):
            q = k * NTC + tc
            hbuf = hb[q % 2]
            for ft in range(8):
                g = q * 8 + ft
                ps = psh[g % 2]
                w = []
                if tc == 0 and ft == 0:
                    w.append((WD, st["wdk"][k]))
                if k == 0 and tc == 0 and ft < 2:
                    w.append((V, self.v_b1T))
                if g >= 2:
                    w.append((A, st["gelA"][g - 2]))

                def mm1(t, tc=tc, ft=ft, ps=ps):
                    last = None
                    for dc in range(DC):
                        last = t.matmul(
                            ps[:, 0:512],
                            w1b[:, dc * FB + ft * 128: dc * FB + (ft + 1) * 128],
                            t_xnT[:, dc * TOK + tc * 512: dc * TOK + (tc + 1) * 512],
                            start=(dc == 0), stop=(dc == DC - 1))
                    return last
                st["mm1T"][g] = p.op("tensor", mm1, waits=w, inc=T)
                w = [(T, st["mm1T"][g])]
                if ft == 0 and q >= 2:
                    w.append((T, st["mm2T"][(q - 2, 3)]))

                def gel(sc, tc=tc, ft=ft, ps=ps, e=e, fb=fb, hbuf=hbuf):
                    return sc.activation(
                        hbuf[:, ft * TCH:(ft + 1) * TCH], ps[:, 0:512], AF.Gelu,
                        bias=self.t_b1T[:, (fb * 8 + ft) * NE + e:
                                        (fb * 8 + ft) * NE + e + 1])
                st["gelA"][g] = p.op("scalar", gel, waits=w, inc=A)
            for tt in range(4):
                m = q * 4 + tt
                par = m % 2
                w = [(A, st["gelA"][q * 8 + 7])]
                if m >= 2:
                    w.append((A, st["t1A"][m - 2]))
                if k == 0 and tc == 0:
                    w.append((V, v_yinit))

                def mm2(t, tc=tc, tt=tt, par=par, hbuf=hbuf):
                    last = None
                    for nh in range(2):
                        for ft in range(8):
                            last = t.matmul(
                                psy[par * 2 + nh][:, 0:512],
                                hbuf[:, ft * TCH + tt * 128: ft * TCH + (tt + 1) * 128],
                                w2b[:, ft * D + nh * 512: ft * D + (nh + 1) * 512],
                                start=(ft == 0), stop=(ft == 7))
                    return last
                st["mm2T"][(q, tt)] = p.op("tensor", mm2, waits=w, inc=T)
                ttg = tc * 4 + tt
                w = [(T, st["mm2T"][(q, tt)])]
                if m >= 2:
                    w.append((V, st["yV"][m - 2]))

                def t1e(sc, tc=tc, tt=tt, par=par, e=e, ttg=ttg, m=m):
                    sc.activation(
                        t_t1[:, (m % 2) * D:(m % 2) * D + 512],
                        psy[par * 2][:, 0:512], AF.Copy,
                        scale=self.t_G[:, ttg * NE + e: ttg * NE + e + 1])
                    return sc.activation(
                        t_t1[:, (m % 2) * D + 512:(m % 2 + 1) * D],
                        psy[par * 2 + 1][:, 0:512], AF.Copy,
                        scale=self.t_G[:, ttg * NE + e: ttg * NE + e + 1])
                st["t1A"][m] = p.op("scalar", t1e, waits=w, inc=A)
                st["yV"][m] = p.op("vector", lambda v, ttg=ttg, m=m: v.tensor_add(
                    t_y[:, ttg * D:(ttg + 1) * D],
                    t_t1[:, (m % 2) * D:(m % 2 + 1) * D],
                    t_y[:, ttg * D:(ttg + 1) * D]),
                    waits=[(A, st["t1A"][m])], inc=V)

    def finish(self):
        O = self.O
        fin = O.n

        def fw(g, fin=fin):
            if fin > 0:
                g.wait_ge(O.h, fin)
            g.nop()
        self.sec("gpsimd", fw)


def _core_meta(parity):
    tiles = list(range(1, 32, 2)) if parity == 0 else list(range(0, 32, 2))
    qpos = (np.array(tiles, dtype=np.float32)[None, :] * 128
            + np.arange(128, dtype=np.float32)[:, None])
    tri = np.where(np.arange(128)[:, None] > np.arange(128)[None, :], NEG, 0.0
                   ).astype(np.float32)
    if parity == 0:
        masks = np.concatenate([np.zeros((128, 128), np.float32), tri], axis=1)
    else:
        masks = np.concatenate([tri, np.full((128, 128), NEG, np.float32)], axis=1)
    return tiles, qpos, masks


def make_inmaps(inputs):
    x = np.asarray(inputs["L0_attn_x"], dtype=np.float32)
    in_maps = []
    for c in range(8):
        bb, parity = c // 2, c % 2
        tiles, qpos, masks = _core_meta(parity)
        xb = np.ascontiguousarray(x[bb])
        xo = np.ascontiguousarray(xb.reshape(32, 128, D)[tiles].reshape(TOK, D))
        m = {
            "xb": xb, "xo": xo, "qpos": qpos, "masks": masks,
            "inw": np.ascontiguousarray(np.asarray(inputs["L0_attn_in_w"], np.float32)),
            "outw": np.ascontiguousarray(np.asarray(inputs["L0_attn_out_w"], np.float32)),
            "ln1g": np.asarray(inputs["L0_attn_ln1_g"], np.float32).reshape(1, D),
            "ln1b": np.asarray(inputs["L0_attn_ln1_b"], np.float32).reshape(1, D),
            "ln2g": np.asarray(inputs["L0_attn_ln2_g"], np.float32).reshape(1, D),
            "ln2b": np.asarray(inputs["L0_attn_ln2_b"], np.float32).reshape(1, D),
            "slopes": np.asarray(inputs["L0_attn_slopes"], np.float32).reshape(1, H),
            "smear": np.asarray(inputs["L0_attn_smear"], np.float32).reshape(1, H),
            "rw": np.ascontiguousarray(np.asarray(inputs["L0_ffn_router_w"], np.float32)),
            "rb": np.asarray(inputs["L0_ffn_router_b"], np.float32).reshape(1, NE),
            "w1": np.ascontiguousarray(np.asarray(inputs["L0_ffn_w1"], np.float32)),
            "b1": np.ascontiguousarray(np.asarray(inputs["L0_ffn_b1"], np.float32)),
            "w2": np.ascontiguousarray(np.asarray(inputs["L0_ffn_w2"], np.float32)),
            "b2": np.ascontiguousarray(np.asarray(inputs["L0_ffn_b2"], np.float32)),
        }
        in_maps.append(m)
    return in_maps


def kernel(**inputs):
    b = B(stage=5)
    nc = b.build()
    in_maps = make_inmaps(inputs)
    res = run_bass_kernel_spmd(nc, in_maps, core_ids=list(range(8)))
    out = np.zeros((4, L, D), dtype=np.float32)
    for c in range(8):
        bb, parity = c // 2, c % 2
        tiles, _, _ = _core_meta(parity)
        yc = np.asarray(res.results[c]["out"])
        for s, t in enumerate(tiles):
            out[bb, t * 128:(t + 1) * 128] = yc[s * 128:(s + 1) * 128]
    return out


if __name__ == "__main__":
    import sys
    stage = int(sys.argv[1]) if len(sys.argv) > 1 else 0
    b = B(stage=stage)
    nc = b.build()
    print("instructions:", nc.n_instructions())
